# version 03 (final)
# baseline (speedup 1.0000x reference)
import math
import numpy as np
import ml_dtypes
import concourse.bass as bass
import concourse.mybir as mybir
from concourse.bass_utils import run_bass_kernel_spmd

F32 = mybir.dt.float32
BF16 = mybir.dt.bfloat16
I32 = mybir.dt.int32
AF = mybir.ActivationFunctionType
ALU = mybir.AluOpType
AX = mybir.AxisListType
BF = ml_dtypes.bfloat16

D = 1024
H = 16
DH = 64
FF = 2816
NCH = D // 128
FCH = FF // 128
R_SPAN = 256.0
EPS_LN = 1e-5
OFF = 384


class Chain:
    def __init__(self, sem):
        self.sem = sem
        self.n = 0

    def b(self, inst, k=1):
        inst.then_inc(self.sem, k)
        self.n += k
        return self.n

    def w(self, eng, n=None):
        v = self.n if n is None else n
        if v > 0:
            eng.wait_ge(self.sem, v)


class KB:

    def __init__(self, NT):
        self.NT = NT
        self.NTH = NT // 2
        self.TT = NT // 128
        self.TTH = self.NTH // 128
        self.DS = NT + 768
        self.KC = NT // 128
        nc = bass.Bass("TRN2", target_bir_lowering=False)
        self.nc = nc

        def inp(name, shape, dt=F32):
            return nc.dram_tensor(name, shape, dt, kind="ExternalInput").ap()

        NTH, DS = self.NTH, self.DS
        self.x0 = inp("x0", [NT, D])
        self.win = {}
        for nm, sh in [("wq0", [D, D]), ("wk0", [D, D]), ("wv0", [D, D]),
                       ("wo0", [D, D]), ("wq1", [D, D]), ("wk1", [D, D]),
                       ("wv1", [D, D]), ("wpost", [D, D]),
                       ("wfr", [D, D]), ("wfi", [D, D]),
                       ("cosm", [D, D]), ("sinm", [D, D]),
                       ("wg", [FF, D]), ("wu", [FF, D]), ("wd", [D, FF])]:
            self.win[nm] = inp(nm, sh)
        self.wob0 = inp("wob0", [1, D])
        self.fbias = inp("fbias", [1, D])
        self.spw = inp("spw", [D, H])
        self.spb = inp("spb", [H, 1])
        self.ident = inp("ident", [128, 128])
        self.dists = inp("dists", [128, DS])
        self.onesr = inp("onesr", [1, 128])
        self.onescol = inp("onescol", [128, 1])
        self.hsel = inp("hsel", [128, 2])
        self.oidx = inp("oidx", [NTH, 1], I32)

        self.out = nc.dram_tensor("out", [NTH, D], F32, kind="ExternalOutput").ap()

        def scrt(name, shape, dt=F32):
            return nc.dram_tensor(name, shape, dt, kind="Internal").ap()

        self.s_x1 = scrt("s_x1", [NT, D])
        self.s_att = scrt("s_att", [NT, D])
        self.s_g = scrt("s_g", [NT, D])
        self.s_xnt1 = scrt("s_xnt1", [NT, D])
        self.s_vt = scrt("s_vt", [NT, D])
        self.s_gate = scrt("s_gate", [NTH, FF])
        self.slab = {}
        for nm, chunks, cols in [("xnf", NCH, NT), ("xnf1", NCH, NT),
                                 ("kf", NCH, NT), ("qf", NCH, NT),
                                 ("xnfo", NCH, NTH), ("qfo", NCH, NTH),
                                 ("attf", NCH, NT),
                                 ("xq", NCH, NTH), ("hq", FCH, NTH),
                                 ("arT", NCH, D), ("naiT", NCH, D),
                                 ("wq0", NCH, D), ("wk0", NCH, D), ("wv0", NCH, D),
                                 ("wo0", NCH, D), ("wq1", NCH, D), ("wk1", NCH, D),
                                 ("wv1", NCH, D), ("wpost", NCH, D),
                                 ("wfr", NCH, D), ("wfi", NCH, D), ("nwfr", NCH, D),
                                 ("cosm", NCH, D), ("sinm", NCH, D),
                                 ("weff", NCH, D),
                                 ("wgq", NCH, FF), ("wuq", NCH, FF), ("wdq", FCH, D)]:
            for hl in range(2):
                self.slab[(nm, hl)] = scrt(f"sl_{nm}_{hl}", [chunks, 128, cols], BF16)

        from contextlib import ExitStack
        self.es = ExitStack()
        es = self.es
        sb = lambda n, s, d=F32: es.enter_context(nc.sbuf_tensor(n, s, d))
        self.sb = sb
        self.identt = sb("identt", [128, 128])
        self.onesrt = sb("onesrt", [1, 128])
        self.onesct = sb("onesct", [128, 1])
        self.hselt = sb("hselt", [128, 2])
        self.spwt = sb("spwt", [128, NCH * H])
        self.spbt = sb("spbt", [H, 1])
        self.wobt = sb("wobt", [128, D])
        self.fbiast = sb("fbiast", [128, D])
        self.zrow = sb("zrow", [128, H])
        self.btile = sb("btile", [128, H])
        self.ebtile = sb("ebtile", [128, H])
        self.hrow = sb("hrow", [1, H])
        self.hcol = sb("hcol", [H, 1])
        self.kmaxc = sb("kmaxc", [H, 1])
        self.qmaxc = sb("qmaxc", [H, 1])
        self.stats = sb("stats", [128, 8])
        self.zmean = sb("zmean", [128, 8])
        self.idxt = sb("idxt", [128, 1], I32)
        self.distt = sb("distt", [128, DS])
        self.mbuf = sb("mbuf", [128, DS])
        self.abuf = sb("abuf", [128, 3392])
        self.bbuf = sb("bbuf", [128, 3392])
        self.cbuf = sb("cbuf", [128, 1024])
        self.wbufA = sb("wbufA", [128, 2816])
        self.hA_hi = sb("hA_hi", [128, 4096], BF16)
        self.hA_lo = sb("hA_lo", [128, 4096], BF16)
        self.hB_hi = sb("hB_hi", [128, 8192], BF16)
        self.hB_lo = sb("hB_lo", [128, 8192], BF16)
        self.wcA_hi = sb("wcA_hi", [128, 1024], BF16)
        self.wcA_lo = sb("wcA_lo", [128, 1024], BF16)
        self.wcB_hi = sb("wcB_hi", [128, 1024], BF16)
        self.wcB_lo = sb("wcB_lo", [128, 1024], BF16)
        self.ob_hi = sb("ob_hi", [128, 512], BF16)
        self.ob_lo = sb("ob_lo", [128, 512], BF16)
        self.kfh_hi = sb("kfh_hi", [64, NT], BF16)
        self.kfh_lo = sb("kfh_lo", [64, NT], BF16)
        self.qfh_hi = sb("qfh_hi", [64, NT], BF16)
        self.qfh_lo = sb("qfh_lo", [64, NT], BF16)
        self.vabuf = sb("vabuf", [128, (NT // 128) * 65])
        self.a_sb = sb("a_sb", [128, 512])
        self.am_sb = sb("am_sb", [128, 512])
        self.ac_sb = sb("ac_sb", [128, 512])
        self.att_sb = sb("att_sb", [128, 64])
        self.den_sb = sb("den_sb", [128, 4])

        self.pb = [es.enter_context(nc.psum_tensor(f"pb{i}", [128, 512], F32))
                   for i in range(8)]

        self.V = Chain(es.enter_context(nc.semaphore("vs")))
        self.S = Chain(es.enter_context(nc.semaphore("ss")))
        self.T = Chain(es.enter_context(nc.semaphore("ts")))
        self.G = Chain(es.enter_context(nc.semaphore("gs")))
        self.dch = {}
        for k in ["misc", "wbufA", "abuf", "bbuf", "cbuf", "hA", "hB", "wcA",
                  "wcB", "ob", "kfh", "qfh", "vabuf", "idxt", "att", "gab",
                  "gbb", "outw"]:
            self.dch[k] = Chain(es.enter_context(nc.semaphore(f"d_{k}")))
        self.chains = ([self.V, self.S, self.T, self.G]
                       + list(self.dch.values()))
        self.block = es.enter_context(nc.Block())

    def bar(self, eng):
        for ch in self.chains:
            ch.w(eng)

    def phase(self, engines, fn):
        nc = self.nc

        def wrap(_):
            e = {"sync": nc.sync, "tensor": nc.tensor, "vector": nc.vector,
                 "scalar": nc.scalar, "gpsimd": nc.gpsimd}
            for name in engines:
                self.bar(e[name])
            fn(e)

        self.block.sync(wrap)

    def hl(self, v, hi, lo, src):
        V = self.V
        V.b(v.tensor_copy(hi, src))
        V.w(v)
        V.b(v.scalar_tensor_tensor(lo, src, 0.0, hi, op0=ALU.add, op1=ALU.subtract))

    def dma(self, eng, key, out, in_):
        self.dch[key].b(eng.dma_start(out=out, in_=in_), 16)

    def dw(self, eng, *keys):
        for k in keys:
            self.dch[k].w(eng)

    def slab_load(self, sync, buf, nm, hl, colsl, w, key, nchunks=NCH):
        self.dma(sync, key,
                 buf[:, 0:nchunks * w].rearrange("p (c w) -> p c w", c=nchunks),
                 self.slab[(nm, hl)].rearrange("c p t -> p c t")[:, :, colsl])

    def slab_store(self, sync, nm, hl, colsl, buf, w, key, nchunks=NCH):
        self.dma(sync, key,
                 self.slab[(nm, hl)].rearrange("c p t -> p c t")[:, :, colsl],
                 buf[:, 0:nchunks * w].rearrange("p (c w) -> p c w", c=nchunks))

    def p0(self):
        kb = self

        def fn(e):
            sync, tensor, vector = e["sync"], e["tensor"], e["vector"]
            T, V = kb.T, kb.V
            for dst, src in [(kb.identt, kb.ident), (kb.onesrt, kb.onesr),
                             (kb.onesct, kb.onescol), (kb.hselt, kb.hsel),
                             (kb.distt, kb.dists), (kb.spbt, kb.spb)]:
                kb.dma(sync, "misc", dst[:, :], src)
            kb.dma(sync, "misc",
                   kb.spwt[:, :].rearrange("p (c h) -> p c h", c=NCH),
                   kb.spw.rearrange("(c p) h -> p c h", p=128))
            kb.dma(sync, "misc", kb.wobt[0:1, :], kb.wob0)
            kb.dma(sync, "misc", kb.fbiast[0:1, :], kb.fbias)
            kb.dw(tensor, "misc")
            for i, buf in enumerate((kb.wobt, kb.fbiast)):
                for hf in range(2):
                    T.b(tensor.matmul(kb.pb[2 * i + hf][:, :], kb.onesrt[0:1, :],
                                      buf[0:1, 512 * hf:512 * hf + 512],
                                      start=True, stop=True))
            T.w(vector)
            for i, buf in enumerate((kb.wobt, kb.fbiast)):
                for hf in range(2):
                    V.b(vector.tensor_copy(buf[:, 512 * hf:512 * hf + 512],
                                           kb.pb[2 * i + hf][:, :]))

        self.phase(["sync", "tensor", "vector"], fn)

    def p1_weight(self, src_ap, rows, nm, negate=False):
        kb = self
        cols = src_ap.shape[1]

        def fn(e):
            sync, vector = e["sync"], e["vector"]
            V = kb.V
            for rt in range(rows // 128):
                V.w(sync)
                kb.dw(sync, "wbufA")
                kb.dma(sync, "wbufA", kb.wbufA[:, 0:cols],
                       src_ap[rt * 128:(rt + 1) * 128, :])
                kb.dw(vector, "wbufA", "hA")
                kb.hl(vector, kb.hA_hi[:, 0:cols], kb.hA_lo[:, 0:cols],
                      kb.wbufA[:, 0:cols])
                if negate:
                    V.w(vector)
                    V.b(vector.tensor_scalar_mul(kb.hA_hi[:, 0:cols],
                                                 kb.hA_hi[:, 0:cols], -1.0))
                    V.b(vector.tensor_scalar_mul(kb.hA_lo[:, 0:cols],
                                                 kb.hA_lo[:, 0:cols], -1.0))
                V.w(sync)
                kb.dma(sync, "hA", kb.slab[(nm, 0)][rt, :, :],
                       kb.hA_hi[:, 0:cols])
                kb.dma(sync, "hA", kb.slab[(nm, 1)][rt, :, :],
                       kb.hA_lo[:, 0:cols])

        self.phase(["sync", "vector"], fn)

    def p1_all(self):
        for nm in ["wq0", "wk0", "wv0", "wo0", "wq1", "wk1", "wv1", "wpost",
                   "wfr", "wfi", "cosm", "sinm"]:
            self.p1_weight(self.win[nm], D, nm)
        self.p1_weight(self.win["wfr"], D, "nwfr", negate=True)

    def p2_weff(self):
        kb = self

        def cmul(lA, lB, rA, rB, dst, scale):
            def fn(e):
                sync, tensor, vector, scalar = (e["sync"], e["tensor"],
                                                e["vector"], e["scalar"])
                T, V, S = kb.T, kb.V, kb.S
                for dt2 in range(2):
                    ds = slice(dt2 * 512, dt2 * 512 + 512)
                    T.w(sync)
                    kb.dw(sync, "hA", "hB")
                    kb.slab_load(sync, kb.hA_hi, rA, 0, ds, 512, "hA")
                    kb.slab_load(sync, kb.hA_lo, rA, 1, ds, 512, "hA")
                    kb.slab_load(sync, kb.hB_hi, rB, 0, ds, 512, "hB")
                    kb.slab_load(sync, kb.hB_lo, rB, 1, ds, 512, "hB")
                    for ot in range(8):
                        osl = slice(ot * 128, ot * 128 + 128)
                        T.w(sync)
                        kb.dw(sync, "wcA", "wcB")
                        kb.slab_load(sync, kb.wcA_hi, lA, 0, osl, 128, "wcA")
                        kb.slab_load(sync, kb.wcA_lo, lA, 1, osl, 128, "wcA")
                        kb.slab_load(sync, kb.wcB_hi, lB, 0, osl, 128, "wcB")
                        kb.slab_load(sync, kb.wcB_lo, lB, 1, osl, 128, "wcB")
                        bank = kb.pb[ot % 2]
                        kb.dw(tensor, "hA", "hB", "wcA", "wcB")
                        V.w(tensor)
                        S.w(tensor)
                        prods = [(kb.wcA_hi, kb.hA_hi), (kb.wcA_lo, kb.hA_hi),
                                 (kb.wcA_hi, kb.hA_lo),
                                 (kb.wcB_hi, kb.hB_hi), (kb.wcB_lo, kb.hB_hi),
                                 (kb.wcB_hi, kb.hB_lo)]
                        n = len(prods) * NCH
                        i = 0
                        for (lw, rw) in prods:
                            for fin in range(NCH):
                                inst = tensor.matmul(
                                    bank[:, :],
                                    lw[:, fin * 128:fin * 128 + 128],
                                    rw[:, fin * 512:fin * 512 + 512],
                                    start=(i == 0), stop=(i == n - 1))
                                i += 1
                        T.b(inst)
                        T.w(scalar)
                        S.b(scalar.mul(kb.cbuf[:, 0:512], bank[:, :], float(scale)))
                        S.w(vector)
                        kb.dw(vector, "ob")
                        kb.hl(vector, kb.ob_hi[:, :], kb.ob_lo[:, :],
                              kb.cbuf[:, 0:512])
                        V.w(sync)
                        kb.dma(sync, "ob", kb.slab[(dst, 0)][ot, :, ds],
                               kb.ob_hi[:, :])
                        kb.dma(sync, "ob", kb.slab[(dst, 1)][ot, :, ds],
                               kb.ob_lo[:, :])

            self.phase(["sync", "tensor", "vector", "scalar"], fn)

        cmul("wfr", "wfi", "cosm", "sinm", "arT", 1.0)
        cmul("wfi", "nwfr", "cosm", "sinm", "naiT", -1.0)
        cmul("arT", "naiT", "cosm", "sinm", "weff", 1.0 / D)

    def quant_ops(self, vector, scalar, dst, src, cols):
        kb = self
        V, S = kb.V, kb.S
        nb = cols // 16
        absv = kb.bbuf[:, 0:cols]
        scr = kb.bbuf[:, cols:cols + 3 * nb]
        V.w(scalar)
        S.b(scalar.activation(absv, src, AF.Abs))
        S.w(vector)
        V.b(vector.tensor_reduce(scr[:, 0:nb],
                                 absv.rearrange("p (c w) -> p c w", w=16),
                                 axis=AX.X, op=ALU.max))
        V.w(vector)
        V.b(vector.tensor_scalar(scr[:, nb:2 * nb], scr[:, 0:nb],
                                 1.0 / 6.0, 1e-12, op0=ALU.mult, op1=ALU.max))
        V.w(vector)
        V.b(vector.reciprocal(scr[:, 2 * nb:3 * nb], scr[:, nb:2 * nb]))
        V.w(vector)
        a3 = absv.rearrange("p (c w) -> p c w", w=16)
        r3 = scr[:, 2 * nb:3 * nb].rearrange("p (c o) -> p c o", o=1)
        a3b, r3b = bass.broadcast_tensor_aps(a3, r3)
        V.b(vector.tensor_tensor(a3, a3b, r3b, op=ALU.mult))
        V.w(vector)
        V.b(vector.tensor_scalar(dst, absv, 0.25, None, op0=ALU.is_gt))
        for thr in (0.75, 1.25, 1.75):
            V.w(vector)
            V.b(vector.scalar_tensor_tensor(dst, absv, thr, dst,
                                            op0=ALU.is_gt, op1=ALU.add))
        V.w(vector)
        V.b(vector.tensor_scalar_mul(dst, dst, 0.5))
        for thr in (2.5, 3.5):
            V.w(vector)
            V.b(vector.scalar_tensor_tensor(dst, absv, thr, dst,
                                            op0=ALU.is_gt, op1=ALU.add))
        V.w(vector)
        V.b(vector.tensor_scalar(absv, absv, 5.0, 2.0, op0=ALU.is_gt, op1=ALU.mult))
        V.w(vector)
        V.b(vector.tensor_add(dst, dst, absv))
        V.w(scalar)
        S.b(scalar.activation(absv, src, AF.Sign))
        S.w(vector)
        V.b(vector.tensor_mul(dst, dst, absv))
        V.w(vector)
        d3 = dst.rearrange("p (c w) -> p c w", w=16)
        s3 = scr[:, nb:2 * nb].rearrange("p (c o) -> p c o", o=1)
        d3b, s3b = bass.broadcast_tensor_aps(d3, s3)
        V.b(vector.tensor_tensor(d3, d3b, s3b, op=ALU.mult))

    def tcast(self, e, src_f32, hi_buf, lo_buf, nchunks, bank0=0, nbanks=2):
        kb = self
        tensor, vector = e["tensor"], e["vector"]
        V, T = kb.V, kb.T
        for c in range(nchunks):
            bank = kb.pb[bank0 + (c % nbanks)]
            V.w(tensor)
            T.b(tensor.transpose(bank[:, 0:128],
                                 src_f32[:, c * 128:c * 128 + 128],
                                 kb.identt[:, :]))
            T.w(vector)
            V.b(vector.tensor_copy(hi_buf[:, c * 128:c * 128 + 128], bank[:, 0:128]))
            V.w(vector)
            V.b(vector.scalar_tensor_tensor(
                lo_buf[:, c * 128:c * 128 + 128], bank[:, 0:128], 0.0,
                hi_buf[:, c * 128:c * 128 + 128], op0=ALU.add, op1=ALU.subtract))

    def p2b_quant_weight(self, nm_src, rows, dst_nm):
        kb = self
        src = kb.win[nm_src]
        cols = src.shape[1]

        def fn(e):
            sync, vector, scalar, tensor = (e["sync"], e["vector"],
                                            e["scalar"], e["tensor"])
            V, S, T = kb.V, kb.S, kb.T
            for rt in range(rows // 128):
                V.w(sync); S.w(sync); T.w(sync)
                kb.dw(sync, "wbufA")
                kb.dma(sync, "wbufA", kb.wbufA[:, 0:cols],
                       src[rt * 128:(rt + 1) * 128, :])
                kb.dw(vector, "wbufA")
                kb.dw(scalar, "wbufA")
                kb.quant_ops(vector, scalar, kb.abuf[:, 0:cols],
                             kb.wbufA[:, 0:cols], cols)
                nchunks = cols // 128
                S.w(tensor)
                kb.dw(vector, "hB")
                kb.tcast(e, kb.abuf[:, 0:cols], kb.hB_hi, kb.hB_lo, nchunks)
                V.w(sync)
                tsl = slice(rt * 128, rt * 128 + 128)
                kb.slab_store(sync, dst_nm, 0, tsl, kb.hB_hi, 128, "hB", nchunks)
                kb.slab_store(sync, dst_nm, 1, tsl, kb.hB_lo, 128, "hB", nchunks)

        self.phase(["sync", "vector", "scalar", "tensor"], fn)

    def p2b_all(self):
        self.p2b_quant_weight("wg", FF, "wgq")
        self.p2b_quant_weight("wu", FF, "wuq")
        self.p2b_quant_weight("wd", D, "wdq")

    def p3_ln(self, src_ap, dst_slab_nm, layer, dst_tok=None):
        kb = self
        NT, TT = self.NT, self.TT

        def fn(e):
            sync, vector, scalar, tensor = (e["sync"], e["vector"],
                                            e["scalar"], e["tensor"])
            V, S, T = kb.V, kb.S, kb.T
            for tt in range(TT):
                rows = slice(tt * 128, tt * 128 + 128)
                V.w(sync); S.w(sync); T.w(sync)
                kb.dw(sync, "abuf")
                kb.dma(sync, "abuf", kb.abuf[:, 0:D], src_ap[rows, :])
                kb.dw(vector, "abuf", "bbuf")
                st = kb.stats
                V.b(vector.reduce_sum(st[:, 0:1], kb.abuf[:, 0:D], axis=AX.X))
                V.b(vector.tensor_mul(kb.bbuf[:, 0:D], kb.abuf[:, 0:D],
                                      kb.abuf[:, 0:D]))
                V.w(vector)
                V.b(vector.reduce_sum(st[:, 1:2], kb.bbuf[:, 0:D], axis=AX.X))
                V.w(vector)
                V.b(vector.tensor_scalar_mul(st[:, 2:3], st[:, 0:1], 1.0 / D))
                V.b(vector.tensor_scalar_mul(st[:, 3:4], st[:, 1:2], 1.0 / D))
                V.w(vector)
                V.b(vector.tensor_mul(st[:, 4:5], st[:, 2:3], st[:, 2:3]))
                V.w(vector)
                V.b(vector.tensor_sub(st[:, 5:6], st[:, 3:4], st[:, 4:5]))
                V.w(vector)
                V.b(vector.tensor_scalar_add(st[:, 5:6], st[:, 5:6], EPS_LN))
                V.w(scalar)
                S.b(scalar.activation(st[:, 6:7], st[:, 5:6], AF.Sqrt))
                S.w(vector)
                V.b(vector.reciprocal(st[:, 6:7], st[:, 6:7]))
                V.w(vector)
                V.b(vector.tensor_scalar_mul(st[:, 7:8], st[:, 2:3], -1.0))
                V.w(vector)
                V.b(vector.tensor_mul(st[:, 7:8], st[:, 7:8], st[:, 6:7]))
                V.w(vector)
                V.b(vector.tensor_scalar(kb.bbuf[:, 0:D], kb.abuf[:, 0:D],
                                         st[:, 6:7], st[:, 7:8],
                                         op0=ALU.mult, op1=ALU.add))
                if dst_tok is not None:
                    V.w(sync)
                    kb.dma(sync, "bbuf", dst_tok[rows, :], kb.bbuf[:, 0:D])
                S.w(tensor); S.w(vector)
                kb.dw(vector, "hA")
                kb.tcast(e, kb.bbuf[:, 0:D], kb.hA_hi, kb.hA_lo, NCH)
                V.w(sync)
                tsl = slice(tt * 128, tt * 128 + 128)
                kb.slab_store(sync, dst_slab_nm, 0, tsl, kb.hA_hi, 128, "hA")
                kb.slab_store(sync, dst_slab_nm, 1, tsl, kb.hA_lo, 128, "hA")
                if layer == 0:
                    S.w(tensor)
                    for c in range(NCH):
                        inst = tensor.matmul(kb.pb[7][:, c:c + 1],
                                             kb.bbuf[:, c * 128:c * 128 + 128],
                                             kb.onesct[:, 0:1],
                                             start=(tt == 0 and c == 0),
                                             stop=(tt == TT - 1 and c == NCH - 1))
                    T.b(inst)
            if layer == 0:
                T.w(vector)
                V.b(vector.tensor_scalar_mul(kb.zmean[:, 0:8], kb.pb[7][:, 0:8],
                                             1.0 / NT))

        self.phase(["sync", "vector", "scalar", "tensor"], fn)

    def p4_z(self):
        kb = self
        NT = self.NT

        def fn(e):
            sync, vector, scalar, tensor = (e["sync"], e["vector"],
                                            e["scalar"], e["tensor"])
            V, S, T = kb.V, kb.S, kb.T
            for c in range(NCH):
                inst = tensor.matmul(kb.pb[0][0:H, 0:1],
                                     kb.spwt[:, c * H:(c + 1) * H],
                                     kb.zmean[:, c:c + 1],
                                     start=(c == 0), stop=(c == NCH - 1))
            T.b(inst)
            T.w(vector)
            V.b(vector.tensor_add(kb.hcol[:, 0:1], kb.pb[0][0:H, 0:1],
                                  kb.spbt[:, 0:1]))
            V.w(scalar)
            S.b(scalar.activation(kb.hcol[:, 0:1], kb.hcol[:, 0:1], AF.Sigmoid))
            S.w(vector)
            V.b(vector.tensor_scalar(kb.hcol[:, 0:1], kb.hcol[:, 0:1],
                                     float(NT) / R_SPAN, 1.0,
                                     op0=ALU.mult, op1=ALU.add))
            V.w(tensor)
            T.b(tensor.matmul(kb.pb[1][0:1, 0:H], kb.hcol[:, 0:1],
                              kb.identt[0:H, 0:H], start=True, stop=True))
            T.w(vector)
            V.b(vector.tensor_copy(kb.hrow[0:1, 0:H], kb.pb[1][0:1, 0:H]))
            V.w(tensor)
            T.b(tensor.matmul(kb.pb[2][:, 0:H], kb.onesrt[0:1, :],
                              kb.hrow[0:1, 0:H], start=True, stop=True))
            T.w(vector)
            V.b(vector.tensor_copy(kb.zrow[:, :], kb.pb[2][:, 0:H]))

        self.phase(["sync", "vector", "scalar", "tensor"], fn)

    def p_bounds(self):
        kb = self

        def fn(e):
            sync, vector, scalar, tensor = (e["sync"], e["vector"],
                                            e["scalar"], e["tensor"])
            V, S, T = kb.V, kb.S, kb.T
            for src_c, dst_c in [(kb.qmaxc, 0), (kb.kmaxc, 1)]:
                V.w(tensor)
                T.b(tensor.matmul(kb.pb[0][0:1, dst_c:dst_c + 2],
                                  src_c[0:2, 0:1], kb.identt[0:2, 0:2],
                                  start=True, stop=True))
                T.w(vector)
                V.b(vector.reduce_max(kb.hrow[0:1, dst_c:dst_c + 1],
                                      kb.pb[0][0:1, dst_c:dst_c + 2], axis=AX.X))
            V.w(scalar)
            S.b(scalar.activation(kb.hrow[0:1, 0:2], kb.hrow[0:1, 0:2], AF.Sqrt))
            S.w(vector)
            V.b(vector.tensor_mul(kb.hrow[0:1, 2:3], kb.hrow[0:1, 0:1],
                                  kb.hrow[0:1, 1:2]))
            V.w(vector)
            V.b(vector.tensor_scalar_mul(kb.hrow[0:1, 2:3], kb.hrow[0:1, 2:3],
                                         1.01 / 8.0))
            V.w(tensor)
            T.b(tensor.matmul(kb.pb[1][:, 0:1], kb.onesrt[0:1, :],
                              kb.hrow[0:1, 2:3], start=True, stop=True))
            T.w(vector)
            V.b(vector.tensor_scalar_mul(kb.btile[:, 0:1], kb.pb[1][:, 0:1], -1.0))
            V.w(scalar)
            S.b(scalar.activation(kb.ebtile[:, 0:1], kb.pb[1][:, 0:1], AF.Exp))

        self.phase(["sync", "vector", "scalar", "tensor"], fn)

    def proj_B(self, src_nm, w_nm, dst_nm, tcols, norms_to=None):
        kb = self

        def fn(e):
            sync, tensor, vector, scalar = (e["sync"], e["tensor"],
                                            e["vector"], e["scalar"])
            V, S, T = kb.V, kb.S, kb.T
            if norms_to is not None:
                V.b(vector.memset(norms_to[:, 0:1], 0.0))
            TW = min(512, tcols)
            nts = tcols // TW
            for fo in range(NCH):
                osl = slice(fo * 128, fo * 128 + 128)
                T.w(sync)
                kb.dw(sync, "wcA")
                kb.slab_load(sync, kb.wcA_hi, w_nm, 0, osl, 128, "wcA")
                kb.slab_load(sync, kb.wcA_lo, w_nm, 1, osl, 128, "wcA")
                for ts in range(nts):
                    tsl = slice(ts * TW, ts * TW + TW)
                    T.w(sync)
                    V.w(sync)
                    kb.dw(sync, "hA")
                    kb.slab_load(sync, kb.hA_hi, src_nm, 0, tsl, TW, "hA")
                    kb.slab_load(sync, kb.hA_lo, src_nm, 1, tsl, TW, "hA")
                    bank = kb.pb[ts % 2]
                    kb.dw(tensor, "wcA", "hA")
                    V.w(tensor); S.w(tensor)
                    prods = [(kb.wcA_hi, kb.hA_hi), (kb.wcA_lo, kb.hA_hi),
                             (kb.wcA_hi, kb.hA_lo)]
                    i = 0
                    for (lw, rw) in prods:
                        for fin in range(NCH):
                            inst = tensor.matmul(
                                bank[:, 0:TW], lw[:, fin * 128:fin * 128 + 128],
                                rw[:, fin * TW:fin * TW + TW],
                                start=(i == 0), stop=(i == 3 * NCH - 1))
                            i += 1
                    T.b(inst)
                    T.w(vector)
                    kb.dw(vector, "ob")
                    kb.hl(vector, kb.ob_hi[:, 0:TW], kb.ob_lo[:, 0:TW],
                          bank[:, 0:TW])
                    if norms_to is not None:
                        kb.dw(vector, "cbuf")
                        T.w(vector)
                        V.b(vector.tensor_mul(kb.cbuf[:, 0:TW],
                                              kb.ob_hi[:, 0:TW],
                                              kb.ob_hi[:, 0:TW]))
                        V.w(tensor)
                        T.b(tensor.matmul(kb.pb[2][0:2, 0:TW], kb.hselt[:, :],
                                          kb.cbuf[:, 0:TW], start=True,
                                          stop=True))
                        T.w(vector)
                        V.b(vector.reduce_max(kb.den_sb[0:2, 0:1],
                                              kb.pb[2][0:2, 0:TW], axis=AX.X))
                        V.w(vector)
                        V.b(vector.tensor_max(norms_to[0:2, 0:1],
                                              norms_to[0:2, 0:1],
                                              kb.den_sb[0:2, 0:1]))
                    V.w(sync)
                    kb.dma(sync, "ob", kb.slab[(dst_nm, 0)][fo, :, tsl],
                           kb.ob_hi[:, 0:TW])
                    kb.dma(sync, "ob", kb.slab[(dst_nm, 1)][fo, :, tsl],
                           kb.ob_lo[:, 0:TW])

        self.phase(["sync", "tensor", "vector", "scalar"], fn)

    def proj_A_wres(self, src_nm, w_nm, ntiles, evac, pre=None):
        kb = self

        def fn(e):
            sync, tensor, vector, scalar = (e["sync"], e["tensor"],
                                            e["vector"], e["scalar"])
            V, S, T = kb.V, kb.S, kb.T
            for hl, buf in [(0, kb.hB_hi), (1, kb.hB_lo)]:
                for nb in range(2):
                    kb.dma(sync, "hB",
                           buf[:, nb * NCH * 512:(nb + 1) * NCH * 512].rearrange(
                               "p (c w) -> p c w", c=NCH),
                           kb.slab[(w_nm, hl)].rearrange(
                               "c p t -> p c t")[:, :, nb * 512:nb * 512 + 512])
            for tt in range(ntiles):
                tsl = slice(tt * 128, tt * 128 + 128)
                T.w(sync)
                kb.dw(sync, "wcA")
                kb.slab_load(sync, kb.wcA_hi, src_nm, 0, tsl, 128, "wcA")
                kb.slab_load(sync, kb.wcA_lo, src_nm, 1, tsl, 128, "wcA")
                if pre is not None:
                    pre(e, tt)
                kb.dw(tensor, "hB", "wcA")
                for nb in range(2):
                    bank = kb.pb[nb]
                    V.w(tensor); S.w(tensor)
                    i = 0
                    for (lw, rsel) in [(kb.wcA_hi, 0), (kb.wcA_lo, 0),
                                       (kb.wcA_hi, 1)]:
                        rbuf = kb.hB_hi if rsel == 0 else kb.hB_lo
                        for fin in range(NCH):
                            col = nb * NCH * 512 + fin * 512
                            inst = tensor.matmul(
                                bank[:, :], lw[:, fin * 128:fin * 128 + 128],
                                rbuf[:, col:col + 512],
                                start=(i == 0), stop=(i == 3 * NCH - 1))
                            i += 1
                    T.b(inst)
                    evac(e, tt, nb, bank)

        self.phase(["sync", "tensor", "vector", "scalar", "gpsimd"], fn)

    def p5_l0(self):
        kb = self
        self.proj_B("xnf", "wk0", "kf", self.NT, norms_to=self.kmaxc)
        self.proj_B("xnf", "wq0", "qf", self.NT, norms_to=self.qmaxc)
        self.p_bounds()

        def evac_v(e, tt, nb, bank):
            sync, scalar = e["sync"], e["scalar"]
            kb.T.w(scalar)
            kb.dw(scalar, "cbuf")
            kb.S.b(scalar.copy(kb.cbuf[:, nb * 512:nb * 512 + 512], bank[:, :]))
            kb.S.w(sync)
            kb.dma(sync, "cbuf",
                   kb.s_vt[tt * 128:tt * 128 + 128, nb * 512:nb * 512 + 512],
                   kb.cbuf[:, nb * 512:nb * 512 + 512])

        self.proj_A_wres("xnf", "wv0", self.TT, evac_v)

        def evac_g(e, tt, nb, bank):
            sync, vector, scalar = e["sync"], e["vector"], e["scalar"]
            cs = slice(nb * 512, nb * 512 + 512)
            kb.T.w(vector); kb.S.w(vector)
            kb.dw(vector, "cbuf")
            kb.V.b(vector.tensor_add(kb.cbuf[:, cs], bank[:, :], kb.fbiast[:, cs]))
            kb.V.w(scalar)
            kb.S.b(scalar.activation(kb.cbuf[:, cs], kb.cbuf[:, cs], AF.Silu))
            kb.S.w(sync)
            kb.dma(sync, "cbuf",
                   kb.s_g[tt * 128:tt * 128 + 128, cs], kb.cbuf[:, cs])

        self.proj_A_wres("xnf", "weff", self.TT, evac_g)

    def p_attn(self, layer):
        kb = self
        NT, KC = self.NT, self.KC
        NQ = NT if layer == 0 else self.NTH
        QW = 512 if NQ >= 512 else NQ
        NJ = QW // 128
        q_nm = "qf" if layer == 0 else "qfo"

        def fn(e):
            sync, tensor, vector, scalar, gp = (e["sync"], e["tensor"],
                                                e["vector"], e["scalar"],
                                                e["gpsimd"])
            V, S, T = kb.V, kb.S, kb.T
            for h in range(H):
                hc, hh = h // 2, (h % 2) * 64
                V.w(sync); S.w(sync); T.w(sync)
                kb.dw(sync, "kfh", "qfh", "vabuf")
                kb.dma(sync, "kfh", kb.kfh_hi[:, :],
                       kb.slab[("kf", 0)][hc, hh:hh + 64, :])
                kb.dma(sync, "kfh", kb.kfh_lo[:, :],
                       kb.slab[("kf", 1)][hc, hh:hh + 64, :])
                kb.dma(sync, "qfh", kb.qfh_hi[:, 0:NQ],
                       kb.slab[(q_nm, 0)][hc, hh:hh + 64, :])
                kb.dma(sync, "qfh", kb.qfh_lo[:, 0:NQ],
                       kb.slab[(q_nm, 1)][hc, hh:hh + 64, :])
                va3 = kb.vabuf[:, :].rearrange("p (a c) -> p a c", c=65)
                kb.dma(sync, "vabuf", va3[:, :, 0:64],
                       kb.s_vt.rearrange("(a p) d -> p a d", p=128)[
                           :, :, 64 * h:64 * h + 64])
                kb.dw(vector, "vabuf")
                kb.dw(tensor, "kfh", "qfh", "vabuf")
                V.b(vector.tensor_scalar(va3[:, :, 0:64], va3[:, :, 0:64],
                                         kb.ebtile[:, 0:1], None,
                                         op0=ALU.mult))
                eb3 = kb.ebtile[:, 0:1].rearrange("p (o c) -> p o c", o=1)
                d3, s3 = bass.broadcast_tensor_aps(va3[:, :, 64:65], eb3)
                V.b(vector.tensor_tensor(va3[:, :, 64:65], s3, s3, op=ALU.bypass))
                if layer == 0:
                    V.w(vector)
                    V.b(vector.tensor_scalar(kb.mbuf[:, :], kb.distt[:, :],
                                             -1.0 / R_SPAN, kb.zrow[:, h:h + 1],
                                             op0=ALU.mult, op1=ALU.add))
                    V.w(vector)
                    V.b(vector.tensor_scalar(kb.mbuf[:, :], kb.mbuf[:, :],
                                             0.0, 1.0, op0=ALU.max, op1=ALU.min))
                    V.w(vector)
                    V.b(vector.scalar_tensor_tensor(kb.mbuf[:, :], kb.distt[:, :],
                                                    0.0, kb.mbuf[:, :],
                                                    op0=ALU.is_ge, op1=ALU.mult))
                exp_done = [0] * 2
                for qs in range(NQ // QW):
                    nkc = min(KC, (qs + 1) * QW // 128) if layer == 0 else KC
                    av_done = 0
                    for kc in range(nkc):
                        stb = kb.pb[4 + (kc % 2)]
                        S.w(tensor, exp_done[kc % 2])
                        V.w(tensor)
                        ksl = slice(kc * 128, kc * 128 + 128)
                        qsl = slice(qs * QW, qs * QW + QW)
                        i3 = 0
                        for (lw, rw) in [(kb.kfh_hi, kb.qfh_hi),
                                         (kb.kfh_lo, kb.qfh_hi),
                                         (kb.kfh_hi, kb.qfh_lo)]:
                            inst = tensor.matmul(stb[:, 0:QW], lw[:, ksl],
                                                 rw[:, qsl],
                                                 start=(i3 == 0), stop=(i3 == 2))
                            i3 += 1
                        tmark = T.b(inst)
                        T.w(scalar, tmark)
                        V.w(scalar)
                        smark = S.b(scalar.activation(
                            kb.a_sb[:, 0:QW], stb[:, 0:QW], AF.Exp,
                            bias=kb.btile[:, 0:1], scale=0.125))
                        exp_done[kc % 2] = smark
                        if layer == 0:
                            c0 = qs * QW - kc * 128 + OFF
                            S.w(vector, smark)
                            vm = V.b(vector.tensor_mul(kb.am_sb[:, 0:QW],
                                                       kb.a_sb[:, 0:QW],
                                                       kb.mbuf[:, c0:c0 + QW]))
                            fully_causal = (kc * 128 + 127 <= qs * QW)
                            if not fully_causal:
                                vc = V.b(vector.scalar_tensor_tensor(
                                    kb.ac_sb[:, 0:QW], kb.distt[:, c0:c0 + QW],
                                    0.0, kb.a_sb[:, 0:QW],
                                    op0=ALU.is_ge, op1=ALU.mult))
                                ac = kb.ac_sb
                            else:
                                vc = vm
                                ac = kb.a_sb
                            am = kb.am_sb
                            V.w(tensor, vc)
                        else:
                            am = kb.a_sb
                            ac = None
                            S.w(tensor, smark)
                        for j in range(NJ):
                            if layer == 0 and kc * 128 > qs * QW + j * 128 + 127:
                                continue
                            first = (kc == 0)
                            lastj = (kc == ((qs * QW + j * 128) // 128
                                            if layer == 0 else KC - 1))
                            jsl = slice(j * 128, j * 128 + 128)
                            inst = tensor.matmul(
                                kb.pb[j][:, 0:65], am[:, jsl],
                                kb.vabuf[:, kc * 65:kc * 65 + 65],
                                start=first,
                                stop=(lastj and layer == 1))
                            if layer == 0:
                                inst = tensor.matmul(
                                    kb.pb[j][:, 65:66], ac[:, jsl],
                                    kb.vabuf[:, kc * 65 + 64:kc * 65 + 65],
                                    start=False, stop=lastj)
                        av_done = T.b(inst)
                    for j in range(NJ):
                        T.w(vector, av_done)
                        if layer == 0:
                            V.b(vector.tensor_scalar(kb.den_sb[:, 2:3],
                                                     kb.pb[j][:, 65:66],
                                                     1e-8, None, op0=ALU.mult))
                            V.w(vector)
                            V.b(vector.tensor_add(kb.den_sb[:, 0:1],
                                                  kb.den_sb[:, 2:3],
                                                  kb.pb[j][:, 64:65]))
                        else:
                            V.b(vector.tensor_copy(kb.den_sb[:, 0:1],
                                                   kb.pb[j][:, 64:65]))
                        V.w(vector)
                        V.b(vector.reciprocal(kb.den_sb[:, 1:2], kb.den_sb[:, 0:1]))
                        V.w(vector)
                        if layer == 0:
                            kb.dw(vector, "att")
                            V.b(vector.tensor_scalar(kb.att_sb[:, :],
                                                     kb.pb[j][:, 0:64],
                                                     kb.den_sb[:, 1:2], None,
                                                     op0=ALU.mult))
                            V.w(sync)
                        else:
                            V.w(scalar)
                            kb.dw(scalar, "att")
                            S.b(scalar.activation(kb.att_sb[:, :],
                                                  kb.pb[j][:, 0:64], AF.Gelu,
                                                  scale=kb.den_sb[:, 1:2]))
                            S.w(sync)
                        r0 = qs * QW + j * 128
                        kb.dma(sync, "att",
                               kb.s_att[r0:r0 + 128, 64 * h:64 * h + 64],
                               kb.att_sb[:, :])
                        S.w(tensor)

        self.phase(["sync", "tensor", "vector", "scalar", "gpsimd"], fn)

    def p_attf(self, ntiles):
        kb = self

        def fn(e):
            sync, vector, tensor = e["sync"], e["vector"], e["tensor"]
            V, T = kb.V, kb.T
            for tt in range(ntiles):
                rows = slice(tt * 128, tt * 128 + 128)
                V.w(sync); T.w(sync)
                kb.dw(sync, "abuf")
                kb.dma(sync, "abuf", kb.abuf[:, 0:D], kb.s_att[rows, :])
                kb.dw(tensor, "abuf")
                kb.dw(vector, "abuf", "hA")
                kb.tcast(e, kb.abuf[:, 0:D], kb.hA_hi, kb.hA_lo, NCH)
                V.w(sync)
                kb.slab_store(sync, "attf", 0, rows, kb.hA_hi, 128, "hA")
                kb.slab_store(sync, "attf", 1, rows, kb.hA_lo, 128, "hA")

        self.phase(["sync", "vector", "tensor"], fn)

    def p7_l0(self):
        kb = self
        self.p_attf(self.TT)

        def evac(e, tt, nb, bank):
            sync, vector = e["sync"], e["vector"]
            cs = slice(nb * 512, nb * 512 + 512)
            rows = slice(tt * 128, tt * 128 + 128)
            V, T, S = kb.V, kb.T, kb.S
            V.w(sync)
            kb.dw(sync, "abuf", "bbuf")
            kb.dma(sync, "abuf", kb.abuf[:, cs], kb.s_g[rows, cs])
            kb.dma(sync, "bbuf", kb.bbuf[:, cs], kb.x0[rows, cs])
            T.w(vector)
            kb.dw(vector, "abuf", "bbuf", "cbuf")
            V.b(vector.tensor_add(kb.cbuf[:, cs], bank[:, :], kb.wobt[:, cs]))
            V.w(vector)
            V.b(vector.tensor_mul(kb.cbuf[:, cs], kb.cbuf[:, cs], kb.abuf[:, cs]))
            V.w(vector)
            V.b(vector.tensor_add(kb.cbuf[:, cs], kb.cbuf[:, cs], kb.bbuf[:, cs]))
            V.w(sync)
            kb.dma(sync, "cbuf", kb.s_x1[rows, cs], kb.cbuf[:, cs])

        self.proj_A_wres("attf", "wo0", self.TT, evac)

    def p8b_own(self):
        kb = self

        def fn(e):
            sync, vector, scalar, tensor, gp = (e["sync"], e["vector"],
                                                e["scalar"], e["tensor"],
                                                e["gpsimd"])
            V, S, T, G = kb.V, kb.S, kb.T, kb.G
            for th in range(kb.TTH):
                rows = slice(th * 128, th * 128 + 128)
                V.w(sync); G.w(sync)
                kb.dw(sync, "idxt")
                kb.dma(sync, "idxt", kb.idxt[:, :], kb.oidx[rows, :])
                kb.dw(gp, "idxt", "gab")
                V.w(gp); S.w(gp); T.w(gp)
                kb.dch["gab"].b(gp.indirect_dma_start(
                    kb.abuf[:, 0:D], None, kb.s_xnt1,
                    bass.IndirectOffsetOnAxis(ap=kb.idxt[:, 0:1], axis=0)), 16)
                kb.dw(tensor, "gab")
                kb.dw(vector, "gab", "hA")
                kb.tcast(e, kb.abuf[:, 0:D], kb.hA_hi, kb.hA_lo, NCH)
                V.w(sync)
                kb.slab_store(sync, "xnfo", 0, rows, kb.hA_hi, 128, "hA")
                kb.slab_store(sync, "xnfo", 1, rows, kb.hA_lo, 128, "hA")
                kb.dw(scalar, "gab", "wbufA")
                kb.dw(vector, "wbufA")
                kb.quant_ops(vector, scalar, kb.wbufA[:, 0:D], kb.abuf[:, 0:D], D)
                V.w(tensor); V.w(vector)
                kb.dw(vector, "hA")
                kb.tcast(e, kb.wbufA[:, 0:D], kb.hA_hi, kb.hA_lo, NCH)
                V.w(sync)
                kb.slab_store(sync, "xq", 0, rows, kb.hA_hi, 128, "hA")
                kb.slab_store(sync, "xq", 1, rows, kb.hA_lo, 128, "hA")

        self.phase(["sync", "vector", "scalar", "tensor", "gpsimd"], fn)

    def p9_l1(self):
        kb = self
        self.proj_B("xnf1", "wk1", "kf", self.NT, norms_to=self.kmaxc)
        self.proj_B("xnfo", "wq1", "qfo", self.NTH, norms_to=self.qmaxc)
        self.p_bounds()

        def evac_v(e, tt, nb, bank):
            sync, scalar = e["sync"], e["scalar"]
            kb.T.w(scalar)
            kb.dw(scalar, "cbuf")
            kb.S.b(scalar.copy(kb.cbuf[:, nb * 512:nb * 512 + 512], bank[:, :]))
            kb.S.w(sync)
            kb.dma(sync, "cbuf",
                   kb.s_vt[tt * 128:tt * 128 + 128, nb * 512:nb * 512 + 512],
                   kb.cbuf[:, nb * 512:nb * 512 + 512])

        self.proj_A_wres("xnf1", "wv1", self.TT, evac_v)

    def ffn_blocks(self):
        blocks = []
        c0 = 0
        while c0 < FF:
            cw = min(512, FF - c0)
            blocks.append((c0, cw))
            c0 += cw
        return blocks

    def p9f_gate(self, w_nm, dst, with_h=False):
        kb = self

        def fn(e):
            sync, tensor, vector, scalar = (e["sync"], e["tensor"],
                                            e["vector"], e["scalar"])
            V, S, T = kb.V, kb.S, kb.T
            for (c0, cw) in kb.ffn_blocks():
                csl = slice(c0, c0 + cw)
                T.w(sync)
                kb.dw(sync, "hB")
                kb.slab_load(sync, kb.hB_hi, w_nm, 0, csl, cw, "hB")
                kb.slab_load(sync, kb.hB_lo, w_nm, 1, csl, cw, "hB")
                for tt in range(kb.TTH):
                    rows = slice(tt * 128, tt * 128 + 128)
                    T.w(sync)
                    kb.dw(sync, "wcA")
                    kb.slab_load(sync, kb.wcA_hi, "xq", 0, rows, 128, "wcA")
                    kb.slab_load(sync, kb.wcA_lo, "xq", 1, rows, 128, "wcA")
                    bank = kb.pb[tt % 2]
                    kb.dw(tensor, "hB", "wcA")
                    V.w(tensor); S.w(tensor)
                    i = 0
                    for (lw, rbuf) in [(kb.wcA_hi, kb.hB_hi),
                                       (kb.wcA_lo, kb.hB_hi),
                                       (kb.wcA_hi, kb.hB_lo)]:
                        for fin in range(NCH):
                            inst = tensor.matmul(
                                bank[:, 0:cw], lw[:, fin * 128:fin * 128 + 128],
                                rbuf[:, fin * cw:fin * cw + cw],
                                start=(i == 0), stop=(i == 3 * NCH - 1))
                            i += 1
                    tm = T.b(inst)
                    if not with_h:
                        T.w(scalar, tm)
                        kb.dw(scalar, "cbuf")
                        S.b(scalar.copy(kb.cbuf[:, 0:cw], bank[:, 0:cw]))
                        S.w(sync)
                        kb.dma(sync, "cbuf", dst[rows, csl], kb.cbuf[:, 0:cw])
                    else:
                        V.w(sync); S.w(sync)
                        kb.dw(sync, "abuf")
                        kb.dma(sync, "abuf", kb.abuf[:, 0:cw], kb.s_gate[rows, csl])
                        kb.dw(scalar, "abuf")
                        S.b(scalar.activation(kb.abuf[:, 0:cw], kb.abuf[:, 0:cw],
                                              AF.Silu))
                        T.w(vector, tm); S.w(vector)
                        kb.dw(vector, "wbufA")
                        V.b(vector.tensor_mul(kb.cbuf[:, 0:cw], bank[:, 0:cw],
                                              kb.abuf[:, 0:cw]))
                        kb.quant_ops(vector, scalar, kb.wbufA[:, 0:cw],
                                     kb.cbuf[:, 0:cw], cw)
                        V.w(tensor); V.w(vector)
                        kb.dw(vector, "hA")
                        kb.tcast(e, kb.wbufA[:, 0:cw], kb.hA_hi, kb.hA_lo,
                                 cw // 128, bank0=2)
                        V.w(sync)
                        for hl, buf in [(0, kb.hA_hi), (1, kb.hA_lo)]:
                            kb.dma(sync, "hA",
                                   kb.slab[("hq", hl)].rearrange(
                                       "c p t -> p c t")[:, c0 // 128:c0 // 128 + cw // 128,
                                                         tt * 128:tt * 128 + 128],
                                   buf[:, 0:cw].rearrange("p (c w) -> p c w",
                                                          w=128))

        self.phase(["sync", "tensor", "vector", "scalar"], fn)

    def p9f_down(self):
        kb = self

        def fn(e):
            sync, tensor, vector, scalar = (e["sync"], e["tensor"],
                                            e["vector"], e["scalar"])
            V, S, T = kb.V, kb.S, kb.T
            ntsb = (kb.TTH + 7) // 8
            for nb in range(2):
                csl = slice(nb * 512, nb * 512 + 512)
                for tsb in range(ntsb):
                    t0 = tsb * 8
                    t1 = min(t0 + 8, kb.TTH)
                    T.w(sync); V.w(sync); S.w(sync)
                    for fin in range(FCH):
                        T.w(sync)
                        kb.dw(sync, "hB", "wcA")
                        kb.dma(sync, "hB", kb.hB_hi[:, 0:512],
                               kb.slab[("wdq", 0)][fin, :, csl])
                        kb.dma(sync, "hB", kb.hB_lo[:, 0:512],
                               kb.slab[("wdq", 1)][fin, :, csl])
                        kb.dma(sync, "wcA", kb.wcA_hi[:, 0:(t1 - t0) * 128],
                               kb.slab[("hq", 0)][fin, :, t0 * 128:t1 * 128])
                        kb.dma(sync, "wcA", kb.wcA_lo[:, 0:(t1 - t0) * 128],
                               kb.slab[("hq", 1)][fin, :, t0 * 128:t1 * 128])
                        for t in range(t0, t1):
                            tof = (t - t0) * 128
                            trs = slice(t * 128, t * 128 + 128)
                            bank = kb.pb[t - t0]
                            kb.dw(tensor, "hB", "wcA")
                            for i3, (lw, rbuf) in enumerate(
                                    [(kb.wcA_hi, kb.hB_hi), (kb.wcA_lo, kb.hB_hi),
                                     (kb.wcA_hi, kb.hB_lo)]):
                                inst = tensor.matmul(
                                    bank[:, :], lw[:, tof:tof + 128],
                                    rbuf[:, 0:512],
                                    start=(fin == 0 and i3 == 0),
                                    stop=(fin == FCH - 1 and i3 == 2))
                            if fin == FCH - 1:
                                tm = T.b(inst)
                                T.w(scalar, tm)
                                kb.dw(scalar, "cbuf")
                                S.b(scalar.activation(kb.cbuf[:, 0:512],
                                                      bank[:, :], AF.Silu))
                                S.w(sync)
                                kb.dma(sync, "cbuf", kb.s_g[trs, csl],
                                       kb.cbuf[:, 0:512])
                        if fin != FCH - 1:
                            T.b(inst)

        self.phase(["sync", "tensor", "vector", "scalar"], fn)

    def p11_l1(self):
        kb = self
        self.p_attf(self.TTH)

        def pre(e, tt):
            sync, gp = e["sync"], e["gpsimd"]
            rows = slice(tt * 128, tt * 128 + 128)
            kb.V.w(sync); kb.V.w(gp)
            kb.dw(sync, "idxt")
            kb.dma(sync, "idxt", kb.idxt[:, :], kb.oidx[rows, :])
            kb.dw(gp, "idxt", "gab", "gbb")
            kb.V.w(gp)
            kb.dch["gab"].b(gp.indirect_dma_start(
                kb.abuf[:, 0:D], None, kb.s_xnt1,
                bass.IndirectOffsetOnAxis(ap=kb.idxt[:, 0:1], axis=0)), 16)
            kb.dch["gbb"].b(gp.indirect_dma_start(
                kb.bbuf[:, 0:D], None, kb.s_x1,
                bass.IndirectOffsetOnAxis(ap=kb.idxt[:, 0:1], axis=0)), 16)

        def evac(e, tt, nb, bank):
            sync, vector = e["sync"], e["vector"]
            cs = slice(nb * 512, nb * 512 + 512)
            rows = slice(tt * 128, tt * 128 + 128)
            V, T = kb.V, kb.T
            V.w(sync)
            kb.dw(sync, "cbuf")
            kb.dma(sync, "cbuf", kb.cbuf[:, 0:512], kb.s_g[rows, cs])
            T.w(vector)
            kb.dw(vector, "cbuf", "gab", "gbb", "outw")
            V.b(vector.tensor_add(kb.wbufA[:, 0:512], bank[:, :], kb.abuf[:, cs]))
            V.w(vector)
            V.b(vector.tensor_mul(kb.wbufA[:, 0:512], kb.wbufA[:, 0:512],
                                  kb.cbuf[:, 0:512]))
            V.w(vector)
            V.b(vector.tensor_add(kb.wbufA[:, 0:512], kb.wbufA[:, 0:512],
                                  kb.bbuf[:, cs]))
            V.w(sync)
            kb.dma(sync, "outw", kb.out[rows, cs], kb.wbufA[:, 0:512])

        self.proj_A_wres("attf", "wpost", self.TTH, evac, pre=pre)


def build_program(NT, upto=None):
    kb = KB(NT)
    with kb.es:
        def done():
            _finish(kb)
            return kb.nc
        kb.p0()
        kb.p1_all()
        kb.p2_weff()
        if upto == "weff":
            return done()
        kb.p2b_all()
        if upto == "quantw":
            return done()
        kb.p3_ln(kb.x0, "xnf", layer=0)
        kb.p4_z()
        if upto == "ln0":
            return done()
        kb.p5_l0()
        if upto == "proj0":
            return done()
        kb.p_attn(0)
        if upto == "attn0":
            return done()
        kb.p7_l0()
        if upto == "x1":
            return done()
        kb.p3_ln(kb.s_x1, "xnf1", layer=1, dst_tok=kb.s_xnt1)
        kb.p8b_own()
        kb.p9_l1()
        if upto == "proj1":
            return done()
        kb.p9f_gate("wgq", kb.s_gate, with_h=False)
        kb.p9f_gate("wuq", kb.s_gate, with_h=True)
        kb.p9f_down()
        if upto == "ffn1":
            return done()
        kb.p_attn(1)
        if upto == "attn1":
            return done()
        kb.p11_l1()
        return done()


def _finish(kb):
    def fn(e):
        pass
    kb.phase(["sync"], fn)


def _consts(NT):
    DSpan = NT + 768
    j = np.arange(D)
    jk = (np.outer(j, j) % D).astype(np.float64)
    ang = 2 * np.pi * jk / D
    c = {}
    c["cosm"] = np.cos(ang).astype(np.float32)
    c["sinm"] = np.sin(ang).astype(np.float32)
    c["ident"] = np.eye(128, dtype=np.float32)
    p = np.arange(128)[:, None]
    col = np.arange(DSpan)[None, :]
    c["dists"] = (col - OFF - p).astype(np.float32)
    c["onesr"] = np.ones((1, 128), np.float32)
    c["onescol"] = np.ones((128, 1), np.float32)
    hs = np.zeros((128, 2), np.float32)
    hs[0:64, 0] = 1.0
    hs[64:128, 1] = 1.0
    c["hsel"] = hs
    return c


def make_inmap(inputs, NT, core):
    seq, par = core // 2, core % 2
    NTH = NT // 2
    c = _consts(NT)
    m = dict(c)
    m["x0"] = np.ascontiguousarray(inputs["L0_attn_x"][seq])
    m["wq0"] = inputs["L0_attn_Wq"]; m["wk0"] = inputs["L0_attn_Wk"]
    m["wv0"] = inputs["L0_attn_Wv"]; m["wo0"] = inputs["L0_attn_Wo_w"]
    m["wob0"] = np.ascontiguousarray(inputs["L0_attn_Wo_b"]).reshape(1, D)
    m["spw"] = inputs["L0_attn_span_w"]
    m["spb"] = np.ascontiguousarray(inputs["L0_attn_span_b"]).reshape(H, 1)
    m["wfr"] = inputs["L0_ffn_w_freq_real"]; m["wfi"] = inputs["L0_ffn_w_freq_imag"]
    m["fbias"] = np.ascontiguousarray(inputs["L0_ffn_bias"]).reshape(1, D)
    m["wq1"] = inputs["L1_attn_Wq"]; m["wk1"] = inputs["L1_attn_Wk"]
    m["wv1"] = inputs["L1_attn_Wv"]; m["wpost"] = inputs["L1_attn_Wpost"]
    m["wg"] = inputs["L1_ffn_w_gate"]; m["wu"] = inputs["L1_ffn_w_up"]
    m["wd"] = inputs["L1_ffn_w_down"]
    m["oidx"] = (par * NTH + np.arange(NTH, dtype=np.int32)).reshape(NTH, 1)
    return m


def kernel(**inputs):
    NT = 4096
    nc = build_program(NT)
    in_maps = [make_inmap(inputs, NT, c) for c in range(8)]
    res = run_bass_kernel_spmd(nc, in_maps, core_ids=list(range(8)))
    outp = np.zeros((4, NT, D), dtype=np.float32)
    NTH = NT // 2
    for c in range(8):
        seq, par = c // 2, c % 2
        outp[seq, par * NTH:(par + 1) * NTH, :] = np.asarray(res.results[c]["out"])
    return outp
